# version 03
# baseline (speedup 1.0000x reference)
import math
import numpy as np
import ml_dtypes
from contextlib import ExitStack
import concourse.bass as bass
import concourse.mybir as mybir
from concourse.bass_utils import run_bass_kernel_spmd

F32 = mybir.dt.float32
BF16 = mybir.dt.bfloat16
I16 = mybir.dt.int16
I32 = mybir.dt.int32
AF = mybir.ActivationFunctionType
ALU = mybir.AluOpType
AXX = mybir.AxisListType.X
bf16 = ml_dtypes.bfloat16

D = 2048
SEQ = 4096
SEG = 512
NSEG = 8
NH = 4
HFF = 8192
NKD = D // 128
EPS_LN = 1e-5
EPS_RMS = 1.1920929e-07
THR = 0.5009765625
C1E8 = float(np.float32(bf16(1e-8)))
INV_SCALE = float(np.float32(1.0) / np.float32(math.sqrt(128.0)))
Z0 = 1.0 / 128.0
NCORES = 8


class Ctr:
    def __init__(self, name):
        self.name = name
        self.h = None
        self.n = 0

    def bump(self, k=1):
        self.n += k
        return (self, self.n)


def _build(debug=0, phases=9, cut=None, noscr=False, noexp=False):
    nc = bass.Bass("TRN2", target_bir_lowering=False)

    x_b = nc.dram_tensor("x_b", [SEQ, D], F32, kind="ExternalInput").ap()
    x_res = nc.dram_tensor("x_res", [1024, D], F32, kind="ExternalInput").ap()
    wq_d = nc.dram_tensor("wq_d", [D, 512], F32, kind="ExternalInput").ap()
    wk_d = nc.dram_tensor("wk_d", [D, 512], F32, kind="ExternalInput").ap()
    wv_d = nc.dram_tensor("wv_d", [D, 512], F32, kind="ExternalInput").ap()
    wo_d = nc.dram_tensor("wo_d", [D, D], F32, kind="ExternalInput").ap()
    betas_d = nc.dram_tensor("betas_d", [1, 512], F32, kind="ExternalInput").ap()
    wgu_d = nc.dram_tensor("wgu_d", [2048, D], F32, kind="ExternalInput").ap()
    wp_d = nc.dram_tensor("wp_d", [D, 1024], F32, kind="ExternalInput").ap()
    mask_d = nc.dram_tensor("mask_d", [SEG, SEG], BF16, kind="ExternalInput").ap()
    idn_d = nc.dram_tensor("idn_d", [128, 128], BF16, kind="ExternalInput").ap()
    idnf_d = nc.dram_tensor("idnf_d", [128, 128], F32, kind="ExternalInput").ap()
    o_rows = nc.dram_tensor("o_rows", [1024, D], F32, kind="ExternalOutput").ap()

    ag_in_gu = nc.dram_tensor("ag_in_gu", [2048, 2048], BF16, kind="Internal")
    ag_out_gu = nc.dram_tensor("ag_out_gu", [16384, 2048], BF16, kind="Internal")
    ag_in_p = nc.dram_tensor("ag_in_p", [1024, 2048], BF16, kind="Internal")
    ag_out_p = nc.dram_tensor("ag_out_p", [8192, 2048], BF16, kind="Internal")

    dbg = {}
    if debug >= 1:
        dbg["d_aggu"] = nc.dram_tensor("d_aggu", [16384, 2048], BF16, kind="ExternalOutput").ap()
        dbg["d_agp"] = nc.dram_tensor("d_agp", [8192, 2048], BF16, kind="ExternalOutput").ap()
    if debug >= 2:
        dbg["d_y"] = nc.dram_tensor("d_y", [1024, D], F32, kind="ExternalOutput").ap()
        dbg["d_mt"] = nc.dram_tensor("d_mt", [2048, 1024], BF16, kind="ExternalOutput").ap()

    EQ = {k: [] for k in ("t", "v", "c", "g", "y")}
    T = Ctr("T"); V = Ctr("V"); C = Ctr("C")
    CC = Ctr("CC"); BAR = Ctr("BAR")
    CTRS = [T, V, C, CC, BAR]

    def emit(eng, fn):
        EQ[eng].append(fn)

    def wait(eng, evt):
        if evt is None:
            return
        ctr, val = evt
        emit(eng, lambda e, c_=ctr, v_=val: e.wait_ge(c_.h, v_))

    def waits(eng, evts):
        for evt in evts:
            wait(eng, evt)

    def barrier():
        tgt = BAR.n + 5
        BAR.n = tgt
        for eng in ("t", "v", "c", "g", "y"):
            emit(eng, lambda e: e.nop().then_inc(BAR.h, 1))
            emit(eng, lambda e, v_=tgt: e.wait_ge(BAR.h, v_))

    class Stream:
        def __init__(self, name, q):
            self.q = q
            self.ctr = Ctr(name)

    def dma(stream, out, in_, prio_waits=()):
        q = stream.q
        waits(q, prio_waits)
        if stream.ctr.n:
            wait(q, (stream.ctr, stream.ctr.n))
        evt = stream.ctr.bump(16)
        emit(q, lambda e, o=out, i=in_, c_=stream.ctr:
             e.dma_start(out=o, in_=i).then_inc(c_.h, 16))
        return evt

    def op(eng, fn, inc=False, prio_waits=()):
        ctr = {"t": T, "v": V, "c": C}[eng]
        waits(eng, prio_waits)
        if eng in ("v", "c"):
            if ctr.n > 0:
                wait(eng, (ctr, ctr.n))
            evt = ctr.bump(1)
            emit(eng, lambda e, f=fn, c_=ctr: f(e).then_inc(c_.h, 1))
            return evt
        if inc:
            evt = ctr.bump(1)
            emit(eng, lambda e, f=fn, c_=ctr: f(e).then_inc(c_.h, 1))
            return evt
        emit(eng, lambda e, f=fn: f(e))
        return None

    def t_event():
        wait("t", (T, T.n))
        evt = T.bump(1)
        emit("t", lambda e: e.nop().then_inc(T.h, 1))
        return evt

    def flush(name):
        with nc.Block(name) as blk:
            if EQ["t"]:
                blk.tensor(lambda t, L=EQ["t"][:]: [fn(t) for fn in L])
            if EQ["v"]:
                blk.vector(lambda v, L=EQ["v"][:]: [fn(v) for fn in L])
            if EQ["c"]:
                blk.scalar(lambda s, L=EQ["c"][:]: [fn(s) for fn in L])
            if EQ["g"]:
                blk.gpsimd(lambda g, L=EQ["g"][:]: [fn(g) for fn in L])
            if EQ["y"]:
                blk.sync(lambda s, L=EQ["y"][:]: [fn(s) for fn in L])
        for k in EQ:
            EQ[k].clear()

    st = ExitStack()
    en = st.enter_context

    ST_C = Stream("sC", "y")
    ST_SLAB = Stream("sS", "y")
    ST_WG = Stream("sWG", "y")
    ST_WP = Stream("sWP", "y")
    ST_XR = Stream("sXR", "y")
    ST_O = Stream("sO", "y")
    ST_X = Stream("sX", "g")
    ST_QL = Stream("sQL", "g")
    ST_W = Stream("sW", "g")
    STREAMS = [ST_C, ST_SLAB, ST_WG, ST_WP, ST_XR, ST_O, ST_X, ST_QL, ST_W]

    for ctr in CTRS + [x.ctr for x in STREAMS]:
        ctr.h = en(nc.semaphore(ctr.name))

    idn = en(nc.sbuf_tensor("idn", [128, 128], BF16))
    idnf = en(nc.sbuf_tensor("idnf", [128, 128], F32))
    maskT = en(nc.sbuf_tensor("maskT", [128, 4, SEG], BF16))
    beta_t = en(nc.sbuf_tensor("beta_t", [128, NH, 128], F32))
    onesf = en(nc.sbuf_tensor("onesf", [1, 128], F32))
    bsb = en(nc.sbuf_tensor("bsb", [1, 512], F32))
    zsb = en(nc.sbuf_tensor("zsb", [128, 2048], F32))
    sqs = en(nc.sbuf_tensor("sqs", [128, 2048], F32))
    red4 = en(nc.sbuf_tensor("red4", [128, 4], F32))

    ev_idn = dma(ST_C, idn[:], idn_d)
    ev_idnf = dma(ST_C, idnf[:], idnf_d)
    ev_mask = dma(ST_C, maskT.ap(), mask_d.rearrange("(m p) u -> p m u", p=128))
    ev_bsb = dma(ST_C, bsb[:], betas_d)
    op("v", lambda e_: e_.memset(onesf[:], 1.0))
    ev_ones = op("v", lambda e_: e_.nop(), inc=True)

    s0 = ExitStack()
    e0 = s0.enter_context
    qw_c = [e0(nc.sbuf_tensor(f"qwc{i}", [128, 4096], BF16)) for i in range(2)]
    qwf = e0(nc.sbuf_tensor("qwf", [128, 4096], F32))
    m1 = e0(nc.sbuf_tensor("m1", [128, 4096], F32))
    m2 = e0(nc.sbuf_tensor("m2", [128, 4096], F32))
    qsum = e0(nc.sbuf_tensor("qsum", [128, 64], F32))
    qs = e0(nc.sbuf_tensor("qs", [128, 64], BF16))
    qsf = e0(nc.sbuf_tensor("qsf", [128, 64], F32))
    qt = e0(nc.sbuf_tensor("qt", [128, 64], F32))
    qtn = e0(nc.sbuf_tensor("qtn", [128, 64], F32))
    wfin_gu = e0(nc.sbuf_tensor("wfin_gu", [128, 16, 2048], BF16))
    wfin_p = e0(nc.sbuf_tensor("wfin_p", [128, 16, 1024], BF16))
    asm = [e0(nc.sbuf_tensor(f"asm{i}", [128, 2048], BF16)) for i in range(2)]
    ptq = e0(nc.psum_tensor("ptq", [128, 512], BF16))

    qfree = [None, None]

    def quant_rows(src_dram, n_rows, n_cols, wfin_all, kblk):
        nchunk = n_rows // (128 * kblk)
        ngrp = (kblk * n_cols) // 64
        ev_free = qfree
        ev_done = None
        for ch in range(nchunk):
            par = ch % 2
            w8 = [ev_free[par]] if ev_free[par] else []
            qc = qw_c[par].ap().rearrange("p (k d) -> p k d", k=kblk)[:, :, 0:n_cols]
            ev_load = dma(
                ST_QL, qc,
                src_dram[ch * 128 * kblk:(ch + 1) * 128 * kblk, :]
                .rearrange("(k p) d -> p k d", p=128),
                prio_waits=w8)
            flat = qw_c[par].ap()
            ev_fx = op("v", lambda e_, a=flat: e_.tensor_copy(a.bitcast(I16),
                                                              a.bitcast(I16)),
                       prio_waits=[ev_load])
            ev_red = op("v", lambda e_, a=flat, ng=ngrp: e_.tensor_reduce(
                qsum.ap()[:, 0:ng].rearrange("p (g o) -> p g o", o=1),
                a.rearrange("p (g z) -> p g z", z=64)[:, 0:ng, :],
                AXX, ALU.add, apply_absolute_value=True), inc=True)
            op("v", lambda e_: e_.tensor_copy(qsum.ap().bitcast(I32),
                                              qsum.ap().bitcast(I32)),
               prio_waits=[ev_red])
            op("v", lambda e_: e_.tensor_scalar_mul(qs[:], qsum[:], 1.0 / 64.0))
            op("v", lambda e_: e_.tensor_scalar_max(qs[:], qs[:], C1E8))
            op("v", lambda e_: e_.tensor_copy(qs.ap().bitcast(I16),
                                              qs.ap().bitcast(I16)))
            op("v", lambda e_: e_.tensor_copy(qsf[:], qs[:]))
            op("v", lambda e_: e_.tensor_scalar_mul(qt[:], qsf[:], THR))
            op("v", lambda e_: e_.tensor_scalar_mul(qtn[:], qt[:], -1.0))
            ev_cast = op("c", lambda e_, a=flat: e_.activation(qwf.ap(), a, AF.Copy),
                         inc=True,
                         prio_waits=[ev_fx] + ([ev_done] if ev_done else
                                               [x for x in qfree if x]))
            ng = ngrp
            op("v", lambda e_, ng=ng: e_.tensor_tensor(
                m1.ap().rearrange("p (g z) -> p g z", z=64)[:, 0:ng, :],
                qwf.ap().rearrange("p (g z) -> p g z", z=64)[:, 0:ng, :],
                qt.ap()[:, 0:ng].rearrange("p (g o) -> p g o", o=1)
                .broadcast_to([128, ng, 64]),
                ALU.is_gt), prio_waits=[ev_cast])
            op("v", lambda e_, ng=ng: e_.tensor_tensor(
                m2.ap().rearrange("p (g z) -> p g z", z=64)[:, 0:ng, :],
                qwf.ap().rearrange("p (g z) -> p g z", z=64)[:, 0:ng, :],
                qtn.ap()[:, 0:ng].rearrange("p (g o) -> p g o", o=1)
                .broadcast_to([128, ng, 64]),
                ALU.is_lt))
            op("v", lambda e_: e_.tensor_sub(m1[:], m1[:], m2[:]))
            op("v", lambda e_, ng=ng: e_.tensor_tensor(
                m2.ap().rearrange("p (g z) -> p g z", z=64)[:, 0:ng, :],
                m1.ap().rearrange("p (g z) -> p g z", z=64)[:, 0:ng, :],
                qsf.ap()[:, 0:ng].rearrange("p (g o) -> p g o", o=1)
                .broadcast_to([128, ng, 64]),
                ALU.mult))
            ws = wfin_all.ap()[:, ch * kblk:(ch + 1) * kblk, 0:n_cols] \
                .rearrange("p k d -> p (k d)")
            op("v", lambda e_, ws=ws, a=flat: e_.tensor_sub(ws, m2.ap(), a))
            op("v", lambda e_, ws=ws, a=flat: e_.tensor_add(ws, a, ws))
            op("v", lambda e_, ws=ws: e_.tensor_copy(ws.bitcast(I16),
                                                     ws.bitcast(I16)))
            ev_free[par] = op("v", lambda e_: e_.nop(), inc=True)
            ev_done = ev_free[par]
        return ev_done

    def drain_all():
        for s_ in STREAMS:
            if s_.ctr.n:
                emit(s_.q, lambda e_, c_=s_.ctr: e_.wait_ge(c_.h, c_.n))

    def cutout(tag):
        drain_all()
        flush("cut_" + tag)
        s0.close()
        st.close()
        return nc

    if cut == "none":
        return cutout("none")
    ev_qgu = quant_rows(wgu_d, 2048, 2048, wfin_gu, kblk=2)
    if cut == "qgu":
        if debug >= 1:
            d_wfin = nc.dram_tensor("d_wfin", [2048, 2048], BF16,
                                    kind="ExternalOutput").ap()
            dma(ST_C, d_wfin.rearrange("(k p) d -> p k d", p=128),
                wfin_gu.ap(), prio_waits=[ev_qgu])
        return cutout("qgu")
    ev_qp = quant_rows(wp_d, D, 1024, wfin_p, kblk=4)
    if cut == "qp":
        return cutout("qp")

    ev_asm_free = [None, None]
    ev_slab = []
    ev_ptq_free = None

    def slab(src, n_blocks, n_out_rows, colmap, ag_in, ev_dep):
        nonlocal ev_ptq_free
        for kd in range(n_out_rows):
            par = kd % 2
            for Hq in range((n_blocks + 3) // 4):
                lastt = None
                for hj in range(4):
                    H = Hq * 4 + hj
                    if H >= n_blocks:
                        break
                    w8 = []
                    if hj == 0:
                        if Hq == 0:
                            w8 = [ev_dep, ev_idn]
                        if ev_ptq_free:
                            w8.append(ev_ptq_free)
                    lastt = op("t", lambda e_, H=H, kd=kd, hj=hj: e_.transpose(
                        ptq[:, 128 * hj:128 * (hj + 1)],
                        src(H, kd), idn[:]),
                        inc=(hj == 3 or H == n_blocks - 1), prio_waits=w8)
                for hj in range(4):
                    H = Hq * 4 + hj
                    if H >= n_blocks:
                        break
                    cb = colmap(H)
                    w8 = [lastt] if hj == 0 else []
                    if hj == 0 and ev_asm_free[par] and Hq == 0:
                        w8.append(ev_asm_free[par])
                    ev_ptq_free = op("v", lambda e_, hj=hj, cb=cb, par=par:
                                     e_.tensor_copy(asm[par][:, cb:cb + 128],
                                                    ptq[:, 128 * hj:128 * (hj + 1)]),
                                     inc=(hj == 3 or H == n_blocks - 1),
                                     prio_waits=w8)
            ev_ev = ev_ptq_free
            ev_st = dma(ST_SLAB, ag_in.ap()[128 * kd:128 * (kd + 1), :],
                        asm[par][:], prio_waits=[ev_ev])
            ev_asm_free[par] = ev_st
            ev_slab.append(ev_st)

    slab(lambda H, kd: wfin_gu.ap()[:, H, 128 * kd:128 * (kd + 1)],
         16, 16, lambda H: 256 * (H % 8) + 128 * (H // 8), ag_in_gu, ev_qgu)
    if cut == "sgu":
        return cutout("sgu")
    slab(lambda H, kd: wfin_p.ap()[:, H, 128 * kd:128 * (kd + 1)],
         16, 8, lambda H: 128 * H, ag_in_p, ev_qp)
    if cut == "sp":
        return cutout("sp")

    if phases <= 0:
        drain_all()
        flush("phase0cut")
        s0.close()
        st.close()
        return nc

    waits("g", ev_slab)
    ev_ag_gu = CC.bump(1)
    emit("g", lambda e_: e_.collective_compute(
        "AllGather", ALU.bypass, replica_groups=[list(range(NCORES))],
        ins=[ag_in_gu.ap().opt()], outs=[ag_out_gu.ap().opt()]).then_inc(CC.h, 1))
    wait("g", ev_ag_gu)
    ev_ag_p = CC.bump(1)
    emit("g", lambda e_: e_.collective_compute(
        "AllGather", ALU.bypass, replica_groups=[list(range(NCORES))],
        ins=[ag_in_p.ap().opt()], outs=[ag_out_p.ap().opt()]).then_inc(CC.h, 1))

    if debug >= 1:
        evp = None
        for blk in range(128):
            evl = dma(ST_C, asm[0][:], ag_out_gu.ap()[128 * blk:128 * (blk + 1), :],
                      prio_waits=[ev_ag_gu] + ([evp] if evp else []))
            evp = dma(ST_C, dbg["d_aggu"][128 * blk:128 * (blk + 1), :], asm[0][:],
                      prio_waits=[evl])
        for blk in range(64):
            evl = dma(ST_C, asm[0][:], ag_out_p.ap()[128 * blk:128 * (blk + 1), :],
                      prio_waits=[ev_ag_p, evp])
            evp = dma(ST_C, dbg["d_agp"][128 * blk:128 * (blk + 1), :], asm[0][:],
                      prio_waits=[evl])

    barrier()
    if phases <= 1:
        drain_all()
    flush("phase0")
    s0.close()
    if phases <= 1:
        st.close()
        return nc

    sMT = ExitStack()
    mt_sb = sMT.enter_context(nc.sbuf_tensor("mt_sb", [128, NKD, 1024], BF16))

    sA = ExitStack()
    eA = sA.enter_context
    wq_sb = eA(nc.sbuf_tensor("wq_sb", [128, NKD, 512], BF16))
    wk_sb = eA(nc.sbuf_tensor("wk_sb", [128, NKD, 512], BF16))
    wv_sb = eA(nc.sbuf_tensor("wv_sb", [128, NKD, 512], BF16))
    memz = eA(nc.sbuf_tensor("memz", [128, NH, 129], BF16))
    mem_f = eA(nc.sbuf_tensor("mem_f", [128, NH, 128], F32))
    z_f = eA(nc.sbuf_tensor("z_f", [128, NH], F32))
    xT = eA(nc.sbuf_tensor("xT", [128, NKD, 512], BF16))
    xseg = [eA(nc.sbuf_tensor(f"xseg{i}", [128, 4, 2048], BF16)) for i in range(2)]
    qT_sb = eA(nc.sbuf_tensor("qT_sb", [128, 512], BF16))
    kT_sb = eA(nc.sbuf_tensor("kT_sb", [128, 512], BF16))
    sqT = eA(nc.sbuf_tensor("sqT", [128, 512], BF16))
    skT = eA(nc.sbuf_tensor("skT", [128, 512], BF16))
    skn = eA(nc.sbuf_tensor("skn", [128, 4, 128], BF16))
    tmpA = eA(nc.sbuf_tensor("tmpA", [128, 512], F32))
    tmpB = eA(nc.sbuf_tensor("tmpB", [128, 512], F32))
    v_sb = eA(nc.sbuf_tensor("v_sb", [128, 4, 4 * 129], BF16))
    E_sb = eA(nc.sbuf_tensor("E_sb", [128, 4, 512], BF16))
    att_d = eA(nc.sbuf_tensor("att_d", [128, 128], F32))
    diff = eA(nc.sbuf_tensor("diff", [128, 128], F32))
    att_bf = eA(nc.sbuf_tensor("att_bf", [128, 4, 128], BF16))
    delta = eA(nc.sbuf_tensor("delta", [128, 4, 128], BF16))
    rcol = eA(nc.sbuf_tensor("rcol", [128, 4], F32))
    pqA = eA(nc.psum_tensor("pqA", [128, 512], F32))
    pqB = eA(nc.psum_tensor("pqB", [128, 512], F32))
    psc = eA(nc.psum_tensor("psc", [128, 2048], F32))
    pnd = eA(nc.psum_tensor("pnd", [128, 512], F32))
    ptp = eA(nc.psum_tensor("ptpA", [128, 512], BF16))

    ev_wq = dma(ST_W, wq_sb.ap(), wq_d.rearrange("(k p) c -> p k c", p=128))
    ev_wk = dma(ST_W, wk_sb.ap(), wk_d.rearrange("(k p) c -> p k c", p=128))
    ev_wv = dma(ST_W, wv_sb.ap(), wv_d.rearrange("(k p) c -> p k c", p=128))

    op("v", lambda e_: e_.memset(mem_f.ap().rearrange("p h d -> p (h d)"), 0.0))
    op("v", lambda e_: e_.memset(z_f[:], Z0))
    op("v", lambda e_: e_.memset(memz.ap().rearrange("p h d -> p (h d)"), 0.0))
    for hh in range(NH):
        op("v", lambda e_, hh=hh: e_.memset(memz.ap()[:, hh, 128:129], Z0))
    op("v", lambda e_: e_.memset(v_sb.ap().rearrange("p m c -> p (m c)"), 0.0))
    for m in range(4):
        for hh in range(NH):
            op("v", lambda e_, m=m, hh=hh:
               e_.memset(v_sb.ap()[:, m, 129 * hh + 128:129 * hh + 129], 1.0))
    ev_vinit = op("v", lambda e_: e_.nop(), inc=True)

    for hh in range(NH):
        evb = op("t", lambda e_, hh=hh: e_.matmul(
            pqA[:, 0:128], onesf[0:1, :], bsb[0:1, 128 * hh:128 * (hh + 1)],
            start=True, stop=True), inc=True,
            prio_waits=[ev_bsb, ev_ones] if hh == 0 else ())
        evs = op("c", lambda e_, hh=hh: e_.activation(
            beta_t.ap()[:, hh, :], pqA[:, 0:128], AF.Sigmoid),
            inc=True, prio_waits=[evb])
        wait("t", evs)

    if cut == "beta":
        drain_all()
        flush("cutbeta")
        sA.close()
        sMT.close()
        st.close()
        return nc

    if noscr:
        scr = lambda apx: apx
    else:
        scr = lambda apx: apx.rearrange("p (t r) -> p r t", r=16)

    ev_xld = [None, None]
    ev_xT_free = None
    ev_psc_free = None
    ev_pq_free = None
    ev_pk_free = None
    ev_pnd_free = None
    ev_ptp_free = None
    ev_mt_done = []
    ev_state = None
    ev_sk_prev = None

    _acut = [None]

    def acut(tag):
        if cut == tag:
            drain_all()
            flush("cut_" + tag)
            sA.close()
            sMT.close()
            st.close()
            _acut[0] = True
        return _acut[0]

    ev_xld[0] = dma(ST_X, xseg[0].ap(),
                    x_b[0:SEG, :].rearrange("(m p) d -> p m d", p=128))

    for i in range(NSEG):
        par = i % 2
        if i + 1 < NSEG:
            ev_xld[1 - par] = dma(
                ST_X, xseg[1 - par].ap(),
                x_b[SEG * (i + 1):SEG * (i + 2), :].rearrange("(m p) d -> p m d", p=128),
                prio_waits=[ev_xT_free] if ev_xT_free else ())

        ev_ev = None
        for kd in range(NKD):
            lastt = None
            for m in range(4):
                w8 = []
                if kd == 0 and m == 0:
                    w8 = [ev_xld[par], ev_idn]
                    if ev_ptp_free:
                        w8.append(ev_ptp_free)
                    if ev_xT_free:
                        w8.append(ev_xT_free)
                elif m == 0 and ev_ev is not None:
                    w8 = [ev_ev]
                lastt = op("t", lambda e_, m=m, kd=kd, par=par: e_.transpose(
                    ptp[:, 128 * m:128 * (m + 1)],
                    xseg[par].ap()[:, m, 128 * kd:128 * (kd + 1)], idn[:]),
                    inc=(m == 3), prio_waits=w8)
            ev_ev = op("v", lambda e_, kd=kd: e_.tensor_copy(
                xT.ap()[:, kd, :], ptp.ap()), inc=True, prio_waits=[lastt])
        ev_ptp_free = ev_ev
        ev_xT_ready = ev_ev
        if i == 0 and acut("xtp"):
            return nc

        ev_vev = None
        for m in range(4):
            w8 = [ev_xT_ready, ev_wv, ev_vinit]
            if ev_psc_free and m == 0:
                w8.append(ev_psc_free)
            lastv = None
            for kd in range(NKD):
                lastv = op("t", lambda e_, m=m, kd=kd: e_.matmul(
                    psc[:, 512 * m:512 * (m + 1)],
                    xT.ap()[:, kd, 128 * m:128 * (m + 1)],
                    wv_sb.ap()[:, kd, :],
                    start=(kd == 0), stop=(kd == NKD - 1)),
                    inc=(kd == NKD - 1), prio_waits=w8 if kd == 0 else ())
            ev_vev = op("v", lambda e_, m=m: e_.tensor_copy(
                v_sb.ap()[:, m, :].rearrange("p (h c) -> p h c", c=129)[:, :, 0:128],
                psc.ap()[:, 512 * m:512 * (m + 1)].rearrange("p (h c) -> p h c", c=128)),
                inc=True, prio_waits=[lastv])
        ev_v_ready = ev_vev
        wait("t", ev_v_ready)
        if i == 0 and acut("vproj"):
            return nc

        for hh in range(NH):
            w8 = [ev_xT_ready, ev_wq]
            if ev_pq_free:
                w8.append(ev_pq_free)
            lastq = None
            for kd in range(NKD):
                lastq = op("t", lambda e_, hh=hh, kd=kd: e_.matmul(
                    pqA[:], wq_sb.ap()[:, kd, 128 * hh:128 * (hh + 1)],
                    xT.ap()[:, kd, :],
                    start=(kd == 0), stop=(kd == NKD - 1)),
                    inc=(kd == NKD - 1), prio_waits=w8 if kd == 0 else ())
            if i == 0 and hh == 0 and acut("qmm"):
                return nc
            ev_q1 = op("c", lambda e_: e_.activation(
                qT_sb[:], scr(pqA.ap()), AF.Copy), inc=True, prio_waits=[lastq])
            ev_q2 = op("c", lambda e_: e_.activation(
                tmpB[:], scr(pqA.ap()), AF.Relu), inc=True,
                prio_waits=[ev_sk_prev] if ev_sk_prev else ())
            ev_mnq = op("v", lambda e_: e_.tensor_scalar_min(
                tmpA[:], scr(pqA.ap()), 0.0), prio_waits=[lastq, ev_q2])
            if noexp:
                ev_q3 = op("c", lambda e_: e_.activation(
                    tmpA[:], tmpA[:], AF.Identity), inc=True,
                    prio_waits=[ev_mnq])
            else:
                ev_q3 = op("c", lambda e_: e_.activation(
                    tmpA[:], tmpA[:], AF.Exp), inc=True, prio_waits=[ev_mnq])
            ev_pq_free = ev_q3
            ev_sq = op("v", lambda e_: e_.tensor_add(sqT[:], tmpA[:], tmpB[:]),
                       inc=True, prio_waits=[ev_q2, ev_q3])
            if i == 0 and hh == 0 and acut("qev"):
                return nc

            w8 = [ev_xT_ready, ev_wk]
            if ev_pk_free:
                w8.extend(ev_pk_free)
            lastk = None
            for kd in range(NKD):
                lastk = op("t", lambda e_, hh=hh, kd=kd: e_.matmul(
                    pqB[:], wk_sb.ap()[:, kd, 128 * hh:128 * (hh + 1)],
                    xT.ap()[:, kd, :],
                    start=(kd == 0), stop=(kd == NKD - 1)),
                    inc=(kd == NKD - 1), prio_waits=w8 if kd == 0 else ())
            ev_k1 = op("c", lambda e_: e_.activation(
                kT_sb[:], pqB.ap(), AF.Copy), inc=True, prio_waits=[lastk])
            ev_k2 = op("c", lambda e_: e_.activation(
                tmpB[:], pqB.ap(), AF.Relu), inc=True, prio_waits=[ev_sq])
            ev_mnk = op("v", lambda e_: e_.tensor_scalar_min(
                tmpA[:], pqB.ap(), 0.0), prio_waits=[lastk, ev_k2])
            ev_k3 = op("c", lambda e_: e_.activation(
                tmpA[:], tmpA[:], AF.Exp), inc=True, prio_waits=[ev_mnk])
            ev_pk_free = [ev_k3, ev_mnk]
            ev_sk = op("v", lambda e_: e_.tensor_add(skT[:], tmpA[:], tmpB[:]),
                       inc=True, prio_waits=[ev_k2, ev_k3])
            ev_sk_prev = ev_sk
            if i == 0 and hh == 0 and acut("qk"):
                return nc

            lastt = None
            for mt in range(4):
                w8 = []
                if mt == 0:
                    w8 = [ev_sk]
                    if ev_ptp_free:
                        w8.append(ev_ptp_free)
                lastt = op("t", lambda e_, mt=mt: e_.transpose(
                    ptp[:, 128 * mt:128 * (mt + 1)],
                    skT.ap()[:, 128 * mt:128 * (mt + 1)], idn[:]),
                    inc=(mt == 3), prio_waits=w8)
            ev_skn = op("v", lambda e_: e_.tensor_copy(
                skn.ap().rearrange("p m c -> p (m c)"), ptp.ap()),
                inc=True, prio_waits=[lastt])
            ev_ptp_free = ev_skn

            lastsc = []
            for mt in range(4):
                w8 = [ev_q1, ev_k1] if mt == 0 else []
                lastsc.append(op("t", lambda e_, mt=mt: e_.matmul(
                    psc[:, 512 * mt:512 * (mt + 1)],
                    kT_sb[:, 128 * mt:128 * (mt + 1)], qT_sb[:],
                    start=True, stop=True), inc=True, prio_waits=w8))
            ev_E = []
            for mt in range(4):
                buf = tmpA if mt % 2 == 0 else tmpB
                w8 = [lastsc[mt], ev_mask]
                if mt >= 2:
                    w8.append(ev_E[mt - 2])
                evx = op("c", lambda e_, mt=mt, buf=buf: e_.activation(
                    buf[:], psc.ap()[:, 512 * mt:512 * (mt + 1)],
                    AF.Exp, scale=INV_SCALE), inc=True, prio_waits=w8)
                ev_E.append(op("v", lambda e_, mt=mt, buf=buf: e_.tensor_mul(
                    E_sb.ap()[:, mt, :], buf[:], maskT.ap()[:, mt, :]),
                    inc=True, prio_waits=[evx]))
            ev_psc_free = ev_E[-1]
            if i == 0 and hh == 0 and acut("E"):
                return nc

            ev_att_m = []
            for m in range(4):
                w8 = [ev_E[-1], ev_v_ready]
                if ev_pnd_free:
                    w8.append(ev_pnd_free)
                for mt in range(4):
                    op("t", lambda e_, m=m, mt=mt, hh=hh: e_.matmul(
                        pnd[:, 0:129],
                        E_sb.ap()[:, mt, 128 * m:128 * (m + 1)],
                        v_sb.ap()[:, mt, 129 * hh:129 * (hh + 1)],
                        start=(mt == 0), stop=(mt == 3)),
                        prio_waits=w8 if mt == 0 else ())
                ev_nd = op("t", lambda e_, m=m, hh=hh: e_.matmul(
                    pnd[:, 256:385],
                    sqT[:, 128 * m:128 * (m + 1)], memz.ap()[:, hh, :],
                    start=True, stop=True), inc=True,
                    prio_waits=[ev_sq] + ([ev_state] if ev_state else []))
                ev_r1 = op("v", lambda e_: e_.reciprocal(rcol[:, 0:1],
                                                         pnd[:, 128:129]),
                           inc=True, prio_waits=[ev_nd])
                ev_r2 = op("v", lambda e_: e_.reciprocal(rcol[:, 1:2],
                                                         pnd[:, 384:385]),
                           inc=True)
                op("v", lambda e_: e_.tensor_scalar_mul(
                    att_d[:], pnd[:, 0:128], rcol[:, 0:1]),
                    prio_waits=[ev_r1, ev_r2])
                op("v", lambda e_: e_.scalar_tensor_tensor(
                    diff[:], pnd[:, 256:384], rcol[:, 1:2], att_d[:],
                    ALU.mult, ALU.subtract))
                op("v", lambda e_, hh=hh: e_.tensor_mul(
                    diff[:], diff[:], beta_t.ap()[:, hh, :]))
                ev_cmb = op("v", lambda e_, m=m: e_.tensor_add(
                    att_bf.ap()[:, m, :], att_d[:], diff[:]), inc=True)
                ev_att_m.append(ev_cmb)

                ev_ak = op("t", lambda e_, m=m, hh=hh: e_.matmul(
                    pnd[:, 256:385],
                    skT[:, 128 * m:128 * (m + 1)], memz.ap()[:, hh, :],
                    start=True, stop=True), inc=True,
                    prio_waits=[ev_sk, ev_cmb])
                ev_r3 = op("v", lambda e_: e_.reciprocal(rcol[:, 2:3],
                                                         pnd[:, 384:385]),
                           inc=True, prio_waits=[ev_ak])
                ev_dl = op("v", lambda e_, m=m, hh=hh: e_.scalar_tensor_tensor(
                    delta.ap()[:, m, :], pnd[:, 256:384], rcol[:, 2:3],
                    v_sb.ap()[:, m, 129 * hh:129 * hh + 128],
                    ALU.mult, ALU.subtract), inc=True, prio_waits=[ev_r3])
                ev_pnd_free = ev_dl
            if i == 0 and hh == 0 and acut("ndm"):
                return nc

            w8 = [ev_skn, ev_pnd_free, ev_pq_free, ev_sq]
            lastdm = None
            for mt in range(4):
                lastdm = op("t", lambda e_, mt=mt: e_.matmul(
                    pqA[:, 0:128], skn.ap()[:, mt, :], delta.ap()[:, mt, :],
                    start=(mt == 0), stop=(mt == 3)),
                    inc=(mt == 3), prio_waits=w8 if mt == 0 else ())
            op("v", lambda e_, hh=hh: e_.tensor_sub(
                mem_f.ap()[:, hh, :], mem_f.ap()[:, hh, :], pqA[:, 0:128]),
                prio_waits=[lastdm])
            ev_mcp = op("v", lambda e_, hh=hh: e_.tensor_copy(
                memz.ap()[:, hh, 0:128], mem_f.ap()[:, hh, :]), inc=True)
            ev_pq_free = ev_mcp
            ev_zr = op("v", lambda e_, hh=hh: e_.tensor_reduce(
                rcol.ap()[:, 3:4], skT.ap(), AXX, ALU.add),
                inc=True, prio_waits=[ev_sk])
            op("v", lambda e_, hh=hh: e_.tensor_add(
                z_f.ap()[:, hh:hh + 1], z_f.ap()[:, hh:hh + 1], rcol[:, 3:4]),
                prio_waits=[ev_zr])
            ev_state = op("v", lambda e_, hh=hh: e_.tensor_copy(
                memz.ap()[:, hh, 128:129], z_f.ap()[:, hh:hh + 1]), inc=True)
            if i == 0 and hh == 0 and acut("mem"):
                return nc

            lastat = None
            for m in range(4):
                w8 = [ev_att_m[m]]
                if m == 0 and ev_ptp_free:
                    w8.append(ev_ptp_free)
                lastat = op("t", lambda e_, m=m: e_.transpose(
                    ptp[:, 128 * m:128 * (m + 1)], att_bf.ap()[:, m, :], idn[:]),
                    inc=(m == 3), prio_waits=w8)
            ev_mtc = op("v", lambda e_, hh=hh, i=i: e_.tensor_copy(
                mt_sb.ap()[:, :, 128 * i + 32 * hh:128 * i + 32 * hh + 32]
                .rearrange("p (m r) t -> p m r t", m=4),
                ptp.ap().rearrange("p (m r t) -> p m r t", m=4, r=4)),
                inc=True, prio_waits=[lastat])
            ev_ptp_free = ev_mtc

        ev_mt_done.append(ev_ptp_free)
        ev_xT_free = t_event()
        if cut == f"seg{i}":
            drain_all()
            flush("cutseg")
            sA.close()
            sMT.close()
            st.close()
            return nc

    if debug >= 2:
        evp = None
        for r in range(NKD):
            evp = dma(ST_C, dbg["d_mt"][128 * r:128 * (r + 1), :],
                      mt_sb.ap()[:, r, :],
                      prio_waits=[ev_mt_done[-1]] + ([evp] if evp else []))
    barrier()
    if phases <= 2:
        drain_all()
    flush("phaseA")
    sA.close()
    if phases <= 2:
        sMT.close()
        st.close()
        return nc

    y_sb = en(nc.sbuf_tensor("y_sb", [128, NSEG, 2048], F32, side="right"))
    sA2 = ExitStack()
    e2 = sA2.enter_context
    wo_sb = e2(nc.sbuf_tensor("wo_sb", [128, NKD, 2048], BF16))
    xres_b = [e2(nc.sbuf_tensor(f"xres{i}", [128, 2048], F32)) for i in range(2)]
    pwo = e2(nc.psum_tensor("pwo", [128, 2048], F32))

    ev_wo = None
    for part in range(4):
        ev_wo = dma(ST_W, wo_sb.ap()[:, 4 * part:4 * (part + 1), :],
                    wo_d[512 * part:512 * (part + 1), :]
                    .rearrange("(k p) c -> p k c", p=128))

    ev_xres = [None, None]
    ev_xres_free = [None, None]
    ev_pwo_free = None
    ev_zsb_free = None

    ev_xres[0] = dma(ST_XR, xres_b[0][:], x_res[0:128, :])
    for i in range(NSEG):
        par = i % 2
        if i + 1 < NSEG:
            ev_xres[1 - par] = dma(
                ST_XR, xres_b[1 - par][:], x_res[128 * (i + 1):128 * (i + 2), :],
                prio_waits=[ev_xres_free[1 - par]] if ev_xres_free[1 - par] else ())
        lastw = None
        for rch in range(NKD):
            for nb in range(4):
                w8 = []
                if rch == 0 and nb == 0:
                    w8 = [ev_mt_done[i], ev_wo]
                    if ev_pwo_free:
                        w8.append(ev_pwo_free)
                lastw = op("t", lambda e_, i=i, rch=rch, nb=nb: e_.matmul(
                    pwo[:, 512 * nb:512 * (nb + 1)],
                    mt_sb.ap()[:, rch, 128 * i:128 * (i + 1)],
                    wo_sb.ap()[:, rch, 512 * nb:512 * (nb + 1)],
                    start=(rch == 0), stop=(rch == NKD - 1)),
                    inc=(rch == NKD - 1 and nb == 3), prio_waits=w8)
        w8 = [lastw, ev_xres[par]]
        if ev_zsb_free:
            w8.append(ev_zsb_free)
        op("v", lambda e_, par=par: e_.tensor_add(zsb[:], pwo.ap(), xres_b[par][:]),
           prio_waits=w8)
        ev_z = op("v", lambda e_: e_.nop(), inc=True)
        ev_pwo_free = ev_z
        ev_xres_free[par] = ev_z
        ev_sum = op("v", lambda e_: e_.tensor_reduce(
            red4.ap()[:, 0:1], zsb.ap(), AXX, ALU.add), inc=True, prio_waits=[ev_z])
        ev_sq2 = op("c", lambda e_: e_.activation(
            sqs[:], zsb[:], AF.Square, accum_out=red4.ap()[:, 1:2]), inc=True,
            prio_waits=[ev_z])
        op("v", lambda e_: e_.tensor_scalar_mul(red4[:, 0:1], red4[:, 0:1],
                                                -1.0 / 2048.0), prio_waits=[ev_sum])
        op("v", lambda e_: e_.tensor_scalar_mul(red4[:, 1:2], red4[:, 1:2],
                                                1.0 / 2048.0), prio_waits=[ev_sq2])
        op("v", lambda e_: e_.tensor_tensor(red4[:, 2:3], red4[:, 0:1],
                                            red4[:, 0:1], ALU.mult))
        op("v", lambda e_: e_.tensor_sub(red4[:, 1:2], red4[:, 1:2], red4[:, 2:3]))
        op("v", lambda e_: e_.tensor_scalar_add(red4[:, 1:2], red4[:, 1:2], EPS_LN))
        ev_rc = op("v", lambda e_: e_.reciprocal(red4[:, 2:3], red4[:, 1:2]),
                   inc=True)
        ev_st2 = op("c", lambda e_: e_.activation(red4.ap()[:, 3:4],
                                                  red4.ap()[:, 2:3], AF.Sqrt),
                    inc=True, prio_waits=[ev_rc])
        ev_y = op("v", lambda e_, i=i: e_.tensor_scalar(
            y_sb.ap()[:, i, :], zsb[:], red4.ap()[:, 0:1], red4.ap()[:, 3:4],
            ALU.add, ALU.mult), inc=True, prio_waits=[ev_st2])
        ev_zsb_free = ev_y

    if debug >= 2:
        evp = None
        for i in range(NSEG):
            evp = dma(ST_C, dbg["d_y"][128 * i:128 * (i + 1), :],
                      y_sb.ap()[:, i, :],
                      prio_waits=[ev_zsb_free] + ([evp] if evp else []))

    barrier()
    if phases <= 3:
        drain_all()
    flush("phaseA2")
    sA2.close()
    sMT.close()
    if phases <= 3:
        st.close()
        return nc

    sB = ExitStack()
    eB = sB.enter_context
    yTh = eB(nc.sbuf_tensor("yTh", [128, NKD, 512], BF16))
    ybc = eB(nc.sbuf_tensor("ybc", [128, 2048], BF16))
    slu = [eB(nc.sbuf_tensor(f"slu{i}", [128, 512], BF16)) for i in range(2)]
    h_sb = eB(nc.sbuf_tensor("h_sb", [128, 64, 512], BF16))
    wg_buf = [eB(nc.sbuf_tensor(f"wg{i}", [128, NKD, 256], BF16)) for i in range(2)]
    wp_buf = [eB(nc.sbuf_tensor(f"wpb{i}", [128, 2, 2048], BF16)) for i in range(2)]
    inv4 = eB(nc.sbuf_tensor("inv4", [128, 8], F32))
    g1s = eB(nc.sbuf_tensor("g1s", [128, 128], F32))

    ev_ybc_free = None
    ev_ptB_free = None
    ev_wg_free = [None, None]
    ev_wp_free = [None, None]
    ev_pg_free = [None, None]
    ev_yT = None
    ev_h_done = None
    ev_inv = None

    def phaseB_batch(bb):
        nonlocal ev_ybc_free, ev_ptB_free, ev_yT, ev_h_done, ev_inv, ev_zsb_free
        with (nc.psum_tensor(f"ptB{bb}", [128, 512], BF16) as ptB,
              nc.psum_tensor(f"pg0{bb}", [128, 512], F32) as pg0,
              nc.psum_tensor(f"pg1{bb}", [128, 512], F32) as pg1,
              nc.psum_tensor(f"pu0{bb}", [128, 512], F32) as pu0,
              nc.psum_tensor(f"pu1{bb}", [128, 512], F32) as pu1,
              nc.psum_tensor(f"pG{bb}", [128, 128], F32) as pG):
            pg = [pg0, pg1]
            pu = [pu0, pu1]
            for m in range(4):
                i = 4 * bb + m
                w8 = [ev_ybc_free] if ev_ybc_free else []
                ev_cast = op("c", lambda e_, i=i: e_.activation(
                    ybc[:], y_sb.ap()[:, i, :], AF.Copy), inc=True, prio_waits=w8)
                for kq in range(4):
                    lastt = None
                    for kj in range(4):
                        kd = 4 * kq + kj
                        w8t = []
                        if kj == 0:
                            w8t = [ev_cast]
                            if ev_ptB_free:
                                w8t.append(ev_ptB_free)
                        lastt = op("t", lambda e_, kd=kd, kj=kj, ptB=ptB:
                                   e_.transpose(
                                       ptB[:, 128 * kj:128 * (kj + 1)],
                                       ybc.ap()[:, 128 * kd:128 * (kd + 1)],
                                       idn[:]),
                                   inc=(kj == 3), prio_waits=w8t)
                    ev_ptB_free = op("v", lambda e_, kq=kq, m=m, ptB=ptB:
                                     e_.tensor_copy(
                        yTh.ap()[:, 4 * kq:4 * (kq + 1), 128 * m:128 * (m + 1)],
                        ptB.ap().rearrange("p (k c) -> p k c", k=4)),
                        inc=True, prio_waits=[lastt])
                ev_ybc_free = ev_ptB_free
            ev_yT = ev_ptB_free

            ev_wg = [None, None]
            w8 = [ev_ag_gu] if bb == 0 else []
            if ev_wg_free[0]:
                w8.append(ev_wg_free[0])
            ev_wg[0] = dma(ST_WG, wg_buf[0].ap(),
                           ag_out_gu.ap()[0:2048, 0:256]
                           .rearrange("(k p) c -> p k c", p=128),
                           prio_waits=w8)
            for jp in range(64):
                par = jp % 2
                if jp + 1 < 64:
                    r = (jp + 1) // 8
                    q = (jp + 1) % 8
                    ev_wg[1 - par] = dma(
                        ST_WG, wg_buf[1 - par].ap(),
                        ag_out_gu.ap()[2048 * r:2048 * (r + 1),
                                       256 * q:256 * (q + 1)]
                        .rearrange("(k p) c -> p k c", p=128),
                        prio_waits=[ev_wg_free[1 - par]]
                        if ev_wg_free[1 - par] else ())
                lastg = None
                for kd in range(NKD):
                    w8 = []
                    if kd == 0:
                        w8 = [ev_wg[par], ev_yT]
                        if ev_pg_free[par]:
                            w8.append(ev_pg_free[par])
                    op("t", lambda e_, kd=kd, par=par: e_.matmul(
                        pg[par][:], wg_buf[par].ap()[:, kd, 0:128],
                        yTh.ap()[:, kd, :],
                        start=(kd == 0), stop=(kd == NKD - 1)), prio_waits=w8)
                    lastg = op("t", lambda e_, kd=kd, par=par: e_.matmul(
                        pu[par][:], wg_buf[par].ap()[:, kd, 128:256],
                        yTh.ap()[:, kd, :],
                        start=(kd == 0), stop=(kd == NKD - 1)),
                        inc=(kd == NKD - 1))
                ev_wg_free[par] = lastg
                ev_sl = op("c", lambda e_, par=par: e_.activation(
                    slu[par][:], pg[par][:], AF.Silu), inc=True,
                    prio_waits=[lastg] +
                    ([ev_pg_free[par]] if ev_pg_free[par] else []))
                ev_h = op("v", lambda e_, jp=jp, par=par: e_.tensor_tensor(
                    h_sb.ap()[:, jp, :], slu[par][:], pu[par][:], ALU.mult),
                    inc=True, prio_waits=[ev_sl])
                ev_pg_free[par] = ev_h
                ev_h_done = ev_h
            wait("t", ev_pg_free[0])
            wait("t", ev_pg_free[1])

            ev_gr = None
            for m in range(4):
                lastG = None
                for k in range(64):
                    w8 = [ev_h_done] if k == 0 else []
                    lastG = op("t", lambda e_, m=m, k=k: e_.matmul(
                        pG[:], h_sb.ap()[:, k, 128 * m:128 * (m + 1)],
                        h_sb.ap()[:, k, 128 * m:128 * (m + 1)],
                        start=(k == 0), stop=(k == 63)),
                        inc=(k == 63), prio_waits=w8)
                op("v", lambda e_: e_.tensor_tensor(g1s[:], pG.ap(), idnf[:],
                                                    ALU.mult),
                   prio_waits=[lastG, ev_idnf])
                ev_gr = op("v", lambda e_, m=m, bb=bb: e_.tensor_reduce(
                    inv4.ap()[:, 4 * bb + m:4 * bb + m + 1], g1s.ap(), AXX,
                    ALU.add), inc=True)
                wait("t", ev_gr)
                op("v", lambda e_, m=m, bb=bb: e_.tensor_scalar(
                    inv4[:, 4 * bb + m:4 * bb + m + 1],
                    inv4[:, 4 * bb + m:4 * bb + m + 1],
                    1.0 / 8192.0, EPS_RMS, ALU.mult, ALU.add),
                    prio_waits=[ev_gr])
                ev_rcp = op("v", lambda e_, m=m, bb=bb: e_.reciprocal(
                    inv4[:, 4 * bb + m:4 * bb + m + 1],
                    inv4[:, 4 * bb + m:4 * bb + m + 1]), inc=True)
                op("c", lambda e_, m=m, bb=bb: e_.activation(
                    inv4.ap()[:, 4 * bb + m:4 * bb + m + 1],
                    inv4.ap()[:, 4 * bb + m:4 * bb + m + 1], AF.Sqrt),
                    prio_waits=[ev_rcp])
            ev_inv = op("c", lambda e_: e_.nop(), inc=True)
            ev_b1v = op("v", lambda e_: e_.nop(), inc=True)
        return ev_b1v

    def phaseB_proj(bb, ev_b1v):
        nonlocal ev_zsb_free, ev_ptB_free
        with (nc.psum_tensor(f"po0{bb}", [128, 2048], F32) as po0,
              nc.psum_tensor(f"po1{bb}", [128, 2048], F32) as po1):
            po = [po0, po1]
            for mp in range(2):
                ev_wp = [None, None]
                w8 = [ev_ag_p] if (bb == 0 and mp == 0) else []
                if ev_wp_free[0]:
                    w8.append(ev_wp_free[0])
                ev_wp[0] = dma(ST_WP, wp_buf[0].ap(),
                               ag_out_p.ap()[0:256, :]
                               .rearrange("(k p) c -> p k c", p=128),
                               prio_waits=w8)
                lastp = None
                for kk in range(32):
                    par = kk % 2
                    if kk + 1 < 32:
                        ev_wp[1 - par] = dma(
                            ST_WP, wp_buf[1 - par].ap(),
                            ag_out_p.ap()[256 * (kk + 1):256 * (kk + 2), :]
                            .rearrange("(k p) c -> p k c", p=128),
                            prio_waits=[ev_wp_free[1 - par]]
                            if ev_wp_free[1 - par] else ())
                    for k2 in range(2):
                        k = 2 * kk + k2
                        for mm in range(2):
                            m = 2 * mp + mm
                            for nb in range(4):
                                w8 = []
                                if k2 == 0 and mm == 0 and nb == 0:
                                    w8 = [ev_wp[par], ev_h_done]
                                    if k == 0:
                                        w8 += [ev_b1v, ev_inv]
                                evmm = op("t", lambda e_, k=k, k2=k2, m=m,
                                          nb=nb, mm=mm, par=par: e_.matmul(
                                    po[mm][:, 512 * nb:512 * (nb + 1)],
                                    h_sb.ap()[:, k, 128 * m:128 * (m + 1)],
                                    wp_buf[par].ap()[:, k2,
                                                     512 * nb:512 * (nb + 1)],
                                    start=(k == 0), stop=(k == 63)),
                                    inc=(k2 == 1 and mm == 1 and nb == 3),
                                    prio_waits=w8)
                                if evmm is not None:
                                    lastp = evmm
                    ev_wp_free[par] = lastp
                for mm in range(2):
                    m = 2 * mp + mm
                    i = 4 * bb + m
                    op("v", lambda e_, mm=mm, m=m, i=i, bb=bb:
                       e_.scalar_tensor_tensor(
                           zsb[:], po[mm].ap(),
                           inv4.ap()[:, 4 * bb + m:4 * bb + m + 1],
                           y_sb.ap()[:, i, :], ALU.mult, ALU.add),
                       prio_waits=[lastp, ev_inv] +
                       ([ev_zsb_free] if ev_zsb_free else []))
                    ev_z = op("v", lambda e_: e_.nop(), inc=True)
                    ev_sum = op("v", lambda e_: e_.tensor_reduce(
                        red4.ap()[:, 0:1], zsb.ap(), AXX, ALU.add), inc=True)
                    ev_sq2 = op("c", lambda e_: e_.activation(
                        sqs[:], zsb[:], AF.Square,
                        accum_out=red4.ap()[:, 1:2]),
                        inc=True, prio_waits=[ev_z])
                    op("v", lambda e_: e_.tensor_scalar_mul(
                        red4[:, 0:1], red4[:, 0:1], -1.0 / 2048.0),
                        prio_waits=[ev_sum])
                    op("v", lambda e_: e_.tensor_scalar_mul(
                        red4[:, 1:2], red4[:, 1:2], 1.0 / 2048.0),
                        prio_waits=[ev_sq2])
                    op("v", lambda e_: e_.tensor_tensor(
                        red4[:, 2:3], red4[:, 0:1], red4[:, 0:1], ALU.mult))
                    op("v", lambda e_: e_.tensor_sub(
                        red4[:, 1:2], red4[:, 1:2], red4[:, 2:3]))
                    op("v", lambda e_: e_.tensor_scalar_add(
                        red4[:, 1:2], red4[:, 1:2], EPS_LN))
                    ev_rc = op("v", lambda e_: e_.reciprocal(
                        red4[:, 2:3], red4[:, 1:2]), inc=True)
                    ev_st2 = op("c", lambda e_: e_.activation(
                        red4.ap()[:, 3:4], red4.ap()[:, 2:3], AF.Sqrt),
                        inc=True, prio_waits=[ev_rc])
                    ev_o = op("v", lambda e_: e_.tensor_scalar(
                        sqs[:], zsb[:], red4.ap()[:, 0:1], red4.ap()[:, 3:4],
                        ALU.add, ALU.mult), inc=True, prio_waits=[ev_st2])
                    ev_zsb_free = ev_o
                    ev_store = dma(ST_O, o_rows[128 * i:128 * (i + 1), :],
                                   sqs[:], prio_waits=[ev_o])
                    wait("v", ev_store)
                wait("t", ev_zsb_free)
        ev_ptB_free = ev_zsb_free

    for bb in range(2):
        ev_b1v = phaseB_batch(bb)
        phaseB_proj(bb, ev_b1v)

    drain_all()
    flush("phaseB")
    sB.close()
    st.close()
    return nc


def _host_inputs(L0_attn_x, L0_attn_Wq, L0_attn_Wk, L0_attn_Wv, L0_attn_Wo,
                 L0_attn_betas, L0_ffn_w_gate_up, L0_ffn_w_proj):
    x = np.asarray(L0_attn_x)
    Wq = np.asarray(L0_attn_Wq); Wk = np.asarray(L0_attn_Wk)
    Wv = np.asarray(L0_attn_Wv); Wo = np.asarray(L0_attn_Wo)
    betas = np.asarray(L0_attn_betas).reshape(16, 128)
    Wgu = np.asarray(L0_ffn_w_gate_up); Wp = np.asarray(L0_ffn_w_proj)

    u = np.arange(SEG)
    s_q = 16 * (u % 32) + u // 32
    t = np.arange(SEG)[:, None]
    maskT = (t <= s_q[None, :]).astype(bf16)
    idn = np.eye(128, dtype=np.float32)

    in_maps = []
    for c in range(NCORES):
        b, g = c // 4, c % 4
        xb = np.ascontiguousarray(x[b])
        xres = np.ascontiguousarray(
            xb.reshape(NSEG, 4, 128, D)[:, g].reshape(1024, D))
        wgu_sl = np.ascontiguousarray(
            np.concatenate([Wgu[1024 * c:1024 * (c + 1)],
                            Wgu[HFF + 1024 * c:HFF + 1024 * (c + 1)]], 0))
        wp_sl = np.ascontiguousarray(Wp[:, 1024 * c:1024 * (c + 1)])
        in_maps.append(dict(
            x_b=xb,
            x_res=xres,
            wq_d=np.ascontiguousarray(Wq[:, 512 * g:512 * (g + 1)]),
            wk_d=np.ascontiguousarray(Wk[:, 512 * g:512 * (g + 1)]),
            wv_d=np.ascontiguousarray(Wv[:, 512 * g:512 * (g + 1)]),
            wo_d=Wo,
            betas_d=np.ascontiguousarray(betas[4 * g:4 * (g + 1)].reshape(1, 512)),
            wgu_d=wgu_sl,
            wp_d=wp_sl,
            mask_d=maskT,
            idn_d=idn.astype(bf16),
            idnf_d=idn,
        ))
    return in_maps


def kernel(L0_attn_x, L0_attn_Wq, L0_attn_Wk, L0_attn_Wv, L0_attn_Wo,
           L0_attn_betas, L0_ffn_w_gate_up, L0_ffn_w_proj):
    in_maps = _host_inputs(L0_attn_x, L0_attn_Wq, L0_attn_Wk, L0_attn_Wv,
                           L0_attn_Wo, L0_attn_betas, L0_ffn_w_gate_up,
                           L0_ffn_w_proj)
    nc = _build(debug=0)
    res = run_bass_kernel_spmd(nc, in_maps, core_ids=list(range(NCORES)))
    out = np.empty((2, SEQ, D), dtype=np.float32)
    for c in range(NCORES):
        b, g = c // 4, c % 4
        rows = np.asarray(res.results[c]["o_rows"]).reshape(NSEG, 128, D)
        out.reshape(2, NSEG, 4, 128, D)[b, :, g] = rows
    return out
